# version 02
# baseline (speedup 1.0000x reference)
import numpy as np
import ml_dtypes
import concourse.bass as bass
import concourse.mybir as mybir
from concourse.bass_utils import run_bass_kernel_spmd
from contextlib import ExitStack

F32 = mybir.dt.float32
BF16 = mybir.dt.bfloat16
I32 = mybir.dt.int32
AF = mybir.ActivationFunctionType
ALU = mybir.AluOpType
AX = mybir.AxisListType
bf16 = ml_dtypes.bfloat16

NCORE = 8
B = 2
SEQ = 8192
DM = 1024
NT = 16
TT = 128
NCH = 8
IF = 4096
NPIECE = 4
PIECE = 1024
NEXP = 8
EPS = 1e-5
NTOK = NT * TT

ENGS = ("sync", "gpsimd", "scalar", "vector", "tensor")

_NAME_N = [0]


def _nm(base):
    _NAME_N[0] += 1
    return f"{base}_{_NAME_N[0]}"


class Prog:
    def __init__(self, nc, stack):
        self.nc = nc
        self.stack = stack
        self.sems = {}
        for name in ("pe", "act", "dve", "qs", "qg", "actc", "vec"):
            h = stack.enter_context(nc.semaphore(name))
            self.sems[name] = [h, 0]
        self.bsem = {}
        self.barc = {}
        for e in ENGS:
            self.bsem[e] = stack.enter_context(nc.semaphore("bar_" + e))
            self.barc[e] = 0
        self.fence_sb = stack.enter_context(
            nc.sbuf_tensor(_nm("fence_sb"), [1, 8], F32))

    def inc(self, name, instr, k=1):
        s = self.sems[name]
        instr.then_inc(s[0], k)
        s[1] += k
        return s[1]

    def val(self, name):
        return self.sems[name][1]

    def wait(self, eng, name, v):
        if v > 0:
            eng.wait_ge(self.sems[name][0], v)

    def edge(self, eng, instr):
        self.inc("vec", instr)
        eng.wait_ge(self.sems["vec"][0], self.sems["vec"][1])


def run_phase(P, block, bodies):
    bar0 = dict(P.barc)
    for name in ENGS:
        def body(eng, name=name):
            for o in ENGS:
                if o != name and bar0[o] > 0:
                    eng.wait_ge(P.bsem[o], bar0[o])
            fn = bodies.get(name)
            if fn:
                fn(eng)
            if name == "sync":
                P.wait(eng, "qs", P.val("qs"))
                eng.nop().then_inc(P.bsem[name], 1)
            elif name == "gpsimd":
                P.wait(eng, "qg", P.val("qg"))
                eng.nop().then_inc(P.bsem[name], 1)
            elif fn is None:
                eng.nop().then_inc(P.bsem[name], 1)
            elif name == "vector":
                eng.memset(P.fence_sb[0:1, 0:1], 0.0).then_inc(P.bsem[name], 1)
            elif name == "scalar":
                eng.activation(out=P.fence_sb[0:1, 1:2],
                               in_=P.fence_sb[0:1, 0:1],
                               func=AF.Copy).then_inc(P.bsem[name], 1)
            elif name == "tensor":
                P.wait(eng, "pe", P.val("pe"))
                eng.nop().then_inc(P.bsem[name], 1)
        getattr(block, name)(body)
        P.barc[name] += 1



def ph_load_const(P, block, loads):
    def sync_body(eng):
        for dst, src, q in loads:
            if q == "qs":
                P.inc("qs", eng.dma_start(out=dst, in_=src), 16)

    def gp_body(eng):
        for dst, src, q in loads:
            if q == "qg":
                P.inc("qg", eng.dma_start(out=dst, in_=src), 16)

    run_phase(P, block, {"sync": sync_body, "gpsimd": gp_body})


def ph_load_x(P, block, C, src):
    def sync_body(eng):
        P.inc("qs", eng.dma_start(
            out=C["x"], in_=src.rearrange("(i q) f -> q i f", q=128)), 16)
    run_phase(P, block, {"sync": sync_body})


def ph_store_x(P, block, C, dst):
    def sync_body(eng):
        P.inc("qs", eng.dma_start(
            out=dst.rearrange("(i q) f -> q i f", q=128), in_=C["x"]), 16)
    run_phase(P, block, {"sync": sync_body})


def ph_dequant(P, block, C, fcq, scales, wdeq, ktiles, n_out):
    nc = P.nc
    with ExitStack() as ph:
        qsb = ph.enter_context(nc.sbuf_tensor(_nm("dq_q"), [128, n_out], I32))
        srow = ph.enter_context(nc.sbuf_tensor(_nm("dq_s"), [1, n_out], F32))
        scb = ph.enter_context(nc.sbuf_tensor(_nm("dq_sc"), [128, 512], F32))
        wout = ph.enter_context(nc.sbuf_tensor(_nm("dq_w"), [128, n_out], BF16))
        psb = ph.enter_context(nc.psum_tensor(_nm("dq_ps"), [128, 512], F32))
        nchk = n_out // 512
        qs0, pe0, dve0 = P.val("qs"), P.val("pe"), P.val("dve")

        def sync_body(eng):
            for g in range(ktiles):
                if g > 0:
                    P.wait(eng, "pe", pe0 + g * nchk)
                    P.wait(eng, "dve", dve0 + g * 2 * nchk)
                P.inc("qs", eng.dma_start(out=qsb[:], in_=fcq[g * 128:(g + 1) * 128, :]), 16)
                P.inc("qs", eng.dma_start(out=srow[:], in_=scales[g:g + 1, :]), 16)
                P.wait(eng, "dve", dve0 + (g + 1) * 2 * nchk)
                P.inc("qs", eng.dma_start(out=wdeq[g * 128:(g + 1) * 128, :], in_=wout[:]), 16)

        def te_body(eng):
            for g in range(ktiles):
                P.wait(eng, "qs", qs0 + g * 48 + 32)
                for j in range(nchk):
                    idx = g * nchk + j
                    if idx > 0:
                        P.wait(eng, "dve", dve0 + (g * 2 * nchk + 2 * j - 2) + 1)
                    ins = eng.matmul(psb[:], C["onesf"][0:1, :],
                                     srow[0:1, j * 512:(j + 1) * 512],
                                     start=True, stop=True)
                    P.inc("pe", ins)

        def ve_body(eng):
            for g in range(ktiles):
                for j in range(nchk):
                    P.wait(eng, "pe", pe0 + g * nchk + j + 1)
                    ins = eng.tensor_copy(out=scb[:], in_=psb[:])
                    P.inc("dve", ins)
                    if j == 0 and g > 0:
                        P.wait(eng, "qs", qs0 + (g - 1) * 48 + 48)
                    ins = eng.scalar_tensor_tensor(
                        out=wout[:, j * 512:(j + 1) * 512],
                        in0=qsb[:, j * 512:(j + 1) * 512], scalar=-128.0,
                        in1=scb[:], op0=ALU.add, op1=ALU.mult)
                    P.inc("dve", ins)

        run_phase(P, block, {"sync": sync_body, "tensor": te_body, "vector": ve_body})


def ph_prep(P, block, C, mode):
    nc = P.nc
    with ExitStack() as ph:
        yh = ph.enter_context(nc.sbuf_tensor(_nm("pr_y"), [128, 2, DM], F32))
        ybf = ph.enter_context(nc.sbuf_tensor(_nm("pr_yb"), [128, 2, DM], BF16))
        junk = ph.enter_context(nc.sbuf_tensor(_nm("pr_j"), [128, DM], F32))
        sv = ph.enter_context(nc.sbuf_tensor(_nm("pr_s"), [128, 8], F32))
        psT = ph.enter_context(nc.psum_tensor(_nm("pr_ps"), [128, 2, DM], BF16))
        x, actT = C["x"], C["actT"]
        dve0, act0, pe0 = P.val("dve"), P.val("act"), P.val("pe")
        ndv = 3 if mode == "ffn0" else 2
        nac = 2 if mode == "ffn0" else 1
        NS, NQ, VAR, VEP, RV, RSTD, NMR = range(7)

        def stats(eng, src):
            eng.tensor_reduce(out=sv[:, NS:NS + 1], in_=src, axis=AX.X,
                              op=ALU.add, negate=True)
            eng.scalar_tensor_tensor(out=junk[:], in0=src, scalar=1.0, in1=src,
                                     op0=ALU.mult, op1=ALU.mult,
                                     accum_out=sv[:, NQ:NQ + 1])
            eng.tensor_scalar(out=sv[:, NS:NS + 1], in0=sv[:, NS:NS + 1],
                              scalar1=1.0 / DM, scalar2=None, op0=ALU.mult)
            P.edge(eng, eng.tensor_scalar(
                out=sv[:, NQ:NQ + 1], in0=sv[:, NQ:NQ + 1],
                scalar1=1.0 / DM, scalar2=None, op0=ALU.mult))
            P.edge(eng, eng.tensor_mul(out=sv[:, VAR:VAR + 1],
                                       in0=sv[:, NS:NS + 1],
                                       in1=sv[:, NS:NS + 1]))
            P.edge(eng, eng.tensor_sub(out=sv[:, VAR:VAR + 1],
                                       in0=sv[:, NQ:NQ + 1],
                                       in1=sv[:, VAR:VAR + 1]))
            P.edge(eng, eng.tensor_scalar(
                out=sv[:, VEP:VEP + 1], in0=sv[:, VAR:VAR + 1],
                scalar1=EPS, scalar2=None, op0=ALU.add))
            return eng.reciprocal(out=sv[:, RV:RV + 1], in_=sv[:, VEP:VEP + 1])

        def rstd_y(eng, src, dst):
            ins = eng.activation(out=sv[:, RSTD:RSTD + 1], in_=sv[:, RV:RV + 1],
                                 func=AF.Sqrt)
            P.inc("actc", ins)
            P.wait(eng, "actc", P.val("actc"))
            ins = eng.activation(out=sv[:, NMR:NMR + 1], in_=sv[:, NS:NS + 1],
                                 func=AF.Identity, scale=sv[:, RSTD:RSTD + 1])
            P.inc("actc", ins)
            P.wait(eng, "actc", P.val("actc"))
            return eng.activation(out=dst, in_=src, func=AF.Identity,
                                  scale=sv[:, RSTD:RSTD + 1],
                                  bias=sv[:, NMR:NMR + 1])

        def ve_body(eng):
            for i in range(NT):
                xi = x[:, i, :]
                P.inc("dve", stats(eng, xi))
                if mode == "ffn0":
                    P.wait(eng, "act", act0 + i * nac + 1)
                    yi = yh[:, i % 2, :]
                    eng.scalar_tensor_tensor(out=xi, in0=yi, scalar=0.5, in1=xi,
                                             op0=ALU.mult, op1=ALU.add)
                    P.inc("dve", stats(eng, yi))
                P.wait(eng, "pe", pe0 + (i + 1) * NCH)
                ins = eng.tensor_copy(
                    out=actT[:, :, i * TT:(i + 1) * TT],
                    in_=psT[:, i % 2, :].rearrange("p (c t) -> p c t", c=NCH))
                P.inc("dve", ins)

        def sc_body(eng):
            for i in range(NT):
                xi = x[:, i, :]
                P.wait(eng, "dve", dve0 + i * ndv + 1)
                if i >= 2:
                    P.wait(eng, "pe", pe0 + (i - 1) * NCH)
                if mode == "ffn0":
                    P.inc("act", rstd_y(eng, xi, yh[:, i % 2, :]))
                    P.wait(eng, "dve", dve0 + i * ndv + 2)
                    P.inc("act", rstd_y(eng, yh[:, i % 2, :],
                                        ybf[:, i % 2, :]))
                else:
                    P.inc("act", rstd_y(eng, xi, ybf[:, i % 2, :]))

        def te_body(eng):
            for i in range(NT):
                P.wait(eng, "act", act0 + (i + 1) * nac)
                if i >= 2:
                    P.wait(eng, "dve", dve0 + (i - 1) * ndv)
                for c in range(NCH):
                    ins = eng.transpose(psT[:, i % 2, c * TT:(c + 1) * TT],
                                        ybf[:, i % 2, c * TT:(c + 1) * TT],
                                        C["identb"][:])
                    P.inc("pe", ins)

        run_phase(P, block, {"vector": ve_body, "scalar": sc_body, "tensor": te_body})


def ph_fc(P, block, C, cfg):
    nc = P.nc
    with ExitStack() as ph:
        w1sb = [ph.enter_context(nc.sbuf_tensor(_nm(f"fc_w1_{i}"), [128, NCH, PIECE], BF16))
                for i in range(2)]
        w2sb = [ph.enter_context(nc.sbuf_tensor(_nm(f"fc_w2_{i}"), [128, NCH, PIECE], BF16))
                for i in range(2)]
        haff = ph.enter_context(nc.sbuf_tensor(_nm("fc_h"), [128, 2, NCH, 512], BF16))
        yT = ph.enter_context(nc.sbuf_tensor(_nm("fc_y"), [128, 2, NCH, 512], BF16))
        b2c = ph.enter_context(nc.sbuf_tensor(_nm("fc_b2c"), [1, 1, DM], F32))
        psA = ph.enter_context(nc.psum_tensor(_nm("fc_psA"), [128, 2, 512], F32))
        psB = ph.enter_context(nc.psum_tensor(_nm("fc_psB"), [128, 2, DM], F32))
        actT, x = C["actT"], C["x"]
        nexp = cfg["nexp"]
        qname = "qg" if cfg["castload"] else "qs"
        q0 = P.val(qname)
        pe0, act0, dve0 = P.val("pe"), P.val("act"), P.val("dve")
        NU = nexp * NPIECE
        PEU = 4 * (NCH + 4) + 1
        ACU = 4 * (1 + NCH)
        qcum = []
        qq = 0
        for e in range(nexp):
            for p in range(NPIECE):
                qq += 2 + (1 if p == 0 else 0)
                qcum.append(qq)

        def af_done(u, T):
            return act0 + u * ACU + T * (1 + NCH) + 1

        def ev_done(u, T, c):
            return act0 + u * ACU + T * (1 + NCH) + 1 + (c + 1)

        def fc1_done(u, T, c):
            return pe0 + u * PEU + T * (NCH + 4) + c + 1

        def fc2_done(u, T, tl):
            return pe0 + u * PEU + T * (NCH + 4) + NCH + tl + 1

        def piece_done(u):
            return pe0 + (u + 1) * PEU

        def xadd_done(u, T, tl):
            return dve0 + u * 16 + T * 4 + tl + 1

        def loads_body(eng):
            for e in range(nexp):
                for p in range(NPIECE):
                    u = e * NPIECE + p
                    if u >= 2:
                        P.wait(eng, "pe", piece_done(u - 2))
                    if p == 0:
                        P.inc(qname, eng.dma_start(
                            out=b2c[:, 0, :], in_=cfg["b2src"](e)), 16)
                    P.inc(qname, eng.dma_start(
                        out=w1sb[u % 2][:],
                        in_=cfg["w1src"](e, p).rearrange("(k q) n -> q k n", q=128)), 16)
                    P.inc(qname, eng.dma_start(
                        out=w2sb[u % 2][:],
                        in_=cfg["w2src"](e, p).rearrange("(k q) n -> q k n", q=128)), 16)

        def sc_body(eng):
            for e in range(nexp):
                gam = cfg["gam"](e) if cfg["gam"] else None
                bet = cfg["bet"](e) if cfg["gam"] else None
                b1 = cfg["b1"](e)
                for p in range(NPIECE):
                    u = e * NPIECE + p
                    for T in range(4):
                        lin = u * 4 + T
                        if lin >= 2:
                            pu, pT = divmod(lin - 2, 4)
                            P.wait(eng, "pe", fc1_done(pu, pT, NCH - 1))
                        for k in range(NCH):
                            src = actT[:, k, T * 512:(T + 1) * 512]
                            if gam is not None:
                                ins = eng.activation(out=haff[:, lin % 2, k, :],
                                                     in_=src, func=AF.Identity,
                                                     scale=gam[:, k:k + 1],
                                                     bias=bet[:, k:k + 1])
                            else:
                                ins = eng.activation(out=haff[:, lin % 2, k, :],
                                                     in_=src, func=AF.Copy)
                        P.inc("act", ins)
                        for c in range(NCH):
                            P.wait(eng, "pe", fc1_done(u, T, c))
                            if lin >= 2:
                                pu, pT = divmod(lin - 2, 4)
                                P.wait(eng, "pe", fc2_done(pu, pT, 3))
                            ci = p * NCH + c
                            ins = eng.activation(out=yT[:, lin % 2, c, :],
                                                 in_=psA[:, c % 2, :],
                                                 func=cfg["act_fn"],
                                                 bias=b1[:, ci:ci + 1])
                            P.inc("act", ins)

        def te_body(eng):
            for e in range(nexp):
                for p in range(NPIECE):
                    u = e * NPIECE + p
                    P.wait(eng, qname, q0 + qcum[u] * 16)
                    w1, w2 = w1sb[u % 2], w2sb[u % 2]
                    for T in range(4):
                        lin = u * 4 + T
                        P.wait(eng, "act", af_done(u, T))
                        for c in range(NCH):
                            gidx = lin * NCH + c
                            if gidx >= 2:
                                gu, grem = divmod(gidx - 2, 4 * NCH)
                                gT, gc = divmod(grem, NCH)
                                P.wait(eng, "act", ev_done(gu, gT, gc))
                            for k in range(NCH):
                                ins = eng.matmul(psA[:, c % 2, :],
                                                 w1[:, k, c * 128:(c + 1) * 128],
                                                 haff[:, lin % 2, k, :],
                                                 start=(k == 0), stop=(k == NCH - 1))
                            P.inc("pe", ins)
                        P.wait(eng, "act", ev_done(u, T, NCH - 1))
                        for tl in range(4):
                            bidx = lin * 4 + tl
                            if bidx >= 2:
                                bu, brem = divmod(bidx - 2, 16)
                                bT, btl = divmod(brem, 4)
                                P.wait(eng, "dve", xadd_done(bu, bT, btl))
                            first = True
                            if p == 0:
                                for oc in range(2):
                                    eng.matmul(psB[:, tl % 2, oc * 512:(oc + 1) * 512],
                                               C["onesf"][0:1, :],
                                               b2c[0:1, 0, oc * 512:(oc + 1) * 512],
                                               start=True, stop=False)
                                first = False
                            for oc in range(2):
                                for ic in range(NCH):
                                    ins = eng.matmul(
                                        psB[:, tl % 2, oc * 512:(oc + 1) * 512],
                                        yT[:, lin % 2, ic, tl * 128:(tl + 1) * 128],
                                        w2[:, ic, oc * 512:(oc + 1) * 512],
                                        start=(first and ic == 0),
                                        stop=(ic == NCH - 1))
                            P.inc("pe", ins, 2 if (T == 3 and tl == 3) else 1)

        def ve_body(eng):
            for e in range(nexp):
                for p in range(NPIECE):
                    u = e * NPIECE + p
                    for T in range(4):
                        for tl in range(4):
                            ti = T * 4 + tl
                            P.wait(eng, "pe", fc2_done(u, T, tl))
                            if cfg["comb"][0] == "const":
                                ins = eng.scalar_tensor_tensor(
                                    out=x[:, ti, :], in0=psB[:, tl % 2, :],
                                    scalar=cfg["comb"][1], in1=x[:, ti, :],
                                    op0=ALU.mult, op1=ALU.add)
                            else:
                                cmb = cfg["comb"][1]
                                ins = eng.scalar_tensor_tensor(
                                    out=x[:, ti, :], in0=psB[:, tl % 2, :],
                                    scalar=cmb[:, ti * NEXP + e:ti * NEXP + e + 1],
                                    in1=x[:, ti, :], op0=ALU.mult, op1=ALU.add)
                            P.inc("dve", ins)

        run_phase(P, block, {
            ("gpsimd" if cfg["castload"] else "sync"): loads_body,
            "scalar": sc_body, "tensor": te_body, "vector": ve_body})


def ph_router(P, block, C, rcfg):
    nc = P.nc
    with ExitStack() as ph:
        ntr = ph.enter_context(nc.sbuf_tensor(_nm("rt_n"), [128, 2, 128], BF16))
        lg = ph.enter_context(nc.sbuf_tensor(_nm("rt_l"), [128, NEXP], F32))
        enc = ph.enter_context(nc.sbuf_tensor(_nm("rt_e"), [128, NEXP], F32))
        enc2 = ph.enter_context(nc.sbuf_tensor(_nm("rt_e2"), [128, NEXP], F32))
        eq1 = ph.enter_context(nc.sbuf_tensor(_nm("rt_q1"), [128, NEXP], F32))
        eq2 = ph.enter_context(nc.sbuf_tensor(_nm("rt_q2"), [128, NEXP], F32))
        sm = ph.enter_context(nc.sbuf_tensor(_nm("rt_s"), [128, 8], F32))
        junk = ph.enter_context(nc.sbuf_tensor(_nm("rt_j"), [128, NEXP], F32))
        psL = ph.enter_context(nc.psum_tensor(_nm("rt_ps"), [128, NEXP], F32))
        actT = C["actT"]
        pe0, act0, dve0 = P.val("pe"), P.val("act"), P.val("dve")
        M1, M2, DEL, A1, B1c = range(5)

        def sc_body(eng):
            for i in range(NT):
                for k in range(NCH):
                    idx = i * NCH + k
                    if idx >= 2:
                        pi, pk = divmod(idx - 2, NCH)
                        P.wait(eng, "pe", pe0 + pi + (1 if pk == NCH - 1 else 0))
                    ins = eng.activation(out=ntr[:, idx % 2, :],
                                         in_=actT[:, k, i * TT:(i + 1) * TT],
                                         func=AF.Identity,
                                         scale=rcfg["rng"][:, k:k + 1],
                                         bias=rcfg["rnb"][:, k:k + 1])
                    P.inc("act", ins)
                P.wait(eng, "dve", dve0 + i * 3 + 3)
                ins = eng.activation(out=sm[:, A1:A1 + 1], in_=sm[:, DEL:DEL + 1],
                                     func=AF.Sigmoid)
                P.inc("act", ins)

        def te_body(eng):
            for i in range(NT):
                if i > 0:
                    P.wait(eng, "dve", dve0 + (i - 1) * 3 + 1)
                for k in range(NCH):
                    P.wait(eng, "act", act0 + i * (NCH + 1) + k + 1)
                    eng.matmul(psL[:], ntr[:, (i * NCH + k) % 2, :],
                               rcfg["wrt"][:, k, :], start=(k == 0), stop=False)
                ins = eng.matmul(psL[:], C["onesf"][0:1, :], rcfg["rb"][0:1, :],
                                 start=False, stop=True)
                P.inc("pe", ins)

        def ve_body(eng):
            for i in range(NT):
                P.wait(eng, "pe", pe0 + i + 1)
                ins = eng.tensor_copy(out=lg[:], in_=psL[:])
                P.inc("dve", ins)
                P.wait(eng, "dve", dve0 + i * 3 + 1)
                ins = eng.scalar_tensor_tensor(out=enc[:], in0=lg[:],
                                               scalar=float(2 ** 23),
                                               in1=C["iotarev"][:],
                                               op0=ALU.mult, op1=ALU.add)
                P.inc("dve", ins)
                P.wait(eng, "dve", dve0 + i * 3 + 2)
                P.edge(eng, eng.tensor_reduce(out=sm[:, M1:M1 + 1], in_=enc[:],
                                              axis=AX.X, op=ALU.max))
                P.edge(eng, eng.tensor_scalar(out=eq1[:], in0=enc[:],
                                              scalar1=sm[:, M1:M1 + 1],
                                              scalar2=None, op0=ALU.is_ge))
                P.edge(eng, eng.scalar_tensor_tensor(out=enc2[:], in0=eq1[:],
                                                     scalar=-1e30,
                                                     in1=enc[:], op0=ALU.mult,
                                                     op1=ALU.add))
                P.edge(eng, eng.tensor_reduce(out=sm[:, M2:M2 + 1], in_=enc2[:],
                                              axis=AX.X, op=ALU.max))
                P.edge(eng, eng.tensor_scalar(out=eq2[:], in0=enc2[:],
                                              scalar1=sm[:, M2:M2 + 1],
                                              scalar2=None, op0=ALU.is_ge))
                P.edge(eng, eng.scalar_tensor_tensor(
                    out=junk[:], in0=eq1[:], scalar=1.0, in1=lg[:],
                    op0=ALU.mult, op1=ALU.mult, accum_out=sm[:, M1:M1 + 1]))
                P.edge(eng, eng.scalar_tensor_tensor(
                    out=junk[:], in0=eq2[:], scalar=1.0, in1=lg[:],
                    op0=ALU.mult, op1=ALU.mult, accum_out=sm[:, M2:M2 + 1]))
                ins = eng.tensor_sub(out=sm[:, DEL:DEL + 1], in0=sm[:, M1:M1 + 1],
                                     in1=sm[:, M2:M2 + 1])
                P.inc("dve", ins)
                P.wait(eng, "act", act0 + (i + 1) * (NCH + 1))
                P.edge(eng, eng.tensor_scalar(out=sm[:, B1c:B1c + 1],
                                              in0=sm[:, A1:A1 + 1],
                                              scalar1=-0.5, scalar2=0.5,
                                              op0=ALU.mult, op1=ALU.add))
                P.edge(eng, eng.tensor_scalar(out=sm[:, A1:A1 + 1],
                                              in0=sm[:, A1:A1 + 1],
                                              scalar1=0.5, scalar2=None,
                                              op0=ALU.mult))
                cmb = rcfg["cmb"]
                P.edge(eng, eng.tensor_scalar(out=cmb[:, i * NEXP:(i + 1) * NEXP],
                                              in0=eq1[:],
                                              scalar1=sm[:, A1:A1 + 1],
                                              scalar2=None, op0=ALU.mult))
                eng.scalar_tensor_tensor(out=cmb[:, i * NEXP:(i + 1) * NEXP], in0=eq2[:],
                                         scalar=sm[:, B1c:B1c + 1],
                                         in1=cmb[:, i * NEXP:(i + 1) * NEXP],
                                         op0=ALU.mult, op1=ALU.add)

        run_phase(P, block, {"scalar": sc_body, "tensor": te_body, "vector": ve_body})


def ph_qkv(P, block, C, acfg):
    nc = P.nc
    rope = acfg["rope"]
    with ExitStack() as ph:
        wq = ph.enter_context(nc.sbuf_tensor(_nm("qk_wq"), [128, NCH, DM], BF16))
        wk = ph.enter_context(nc.sbuf_tensor(_nm("qk_wk"), [128, NCH, DM], BF16))
        wv = ph.enter_context(nc.sbuf_tensor(_nm("qk_wv"), [128, NCH, DM], BF16))
        qst = ph.enter_context(nc.sbuf_tensor(_nm("qk_qs"), [128, NCH, 512], BF16))
        kst = ph.enter_context(nc.sbuf_tensor(_nm("qk_ks"), [128, NCH, 512], BF16))
        vst = ph.enter_context(nc.sbuf_tensor(_nm("qk_vs"), [128, 4, DM], BF16))
        rotq = ph.enter_context(nc.sbuf_tensor(_nm("qk_rq"), [128, NCH, 512], BF16))
        rotk = ph.enter_context(nc.sbuf_tensor(_nm("qk_rk"), [128, NCH, 512], BF16))
        tsc = ph.enter_context(nc.sbuf_tensor(_nm("qk_tsc"), [128, NCH, 512], BF16))
        cosb = ph.enter_context(nc.sbuf_tensor(_nm("qk_cos"), [128, 2, 512], BF16))
        sinb = ph.enter_context(nc.sbuf_tensor(_nm("qk_sin"), [128, 2, 512], BF16))
        psQ = ph.enter_context(nc.psum_tensor(_nm("qk_psQ"), [128, 2, 512], F32))
        psV = ph.enter_context(nc.psum_tensor(_nm("qk_psV"), [128, 2, DM], F32))
        if rope:
            psSw = ph.enter_context(nc.psum_tensor(_nm("qk_psSw"), [128, 2, 512], F32))
        actT, x = C["actT"], C["x"]
        pe0, act0, dve0, qs0, qg0 = (P.val("pe"), P.val("act"), P.val("dve"),
                                     P.val("qs"), P.val("qg"))
        PE_TB = 36 if rope else 20
        AC_TB = 20
        DV_TB = 18 if rope else 3

        def n_qs_tb(Tb):
            return (2 if rope else 0) + 2

        def qs_at(Tb):
            return qs0 + sum(n_qs_tb(t) for t in range(Tb + 1)) * 16

        def ev_q(Tb, c):
            return act0 + Tb * AC_TB + c + 1

        def ev_k(Tb, c):
            return act0 + Tb * AC_TB + 8 + c + 1

        def ev_v(Tb, tl):
            return act0 + Tb * AC_TB + 16 + tl + 1

        def pe_sw(Tb, sw_i):
            return pe0 + Tb * PE_TB + 20 + sw_i + 1

        def dv_sw(Tb, sw_i):
            return dve0 + Tb * DV_TB + (sw_i + 1 if sw_i < 8 else sw_i + 2)

        def dv_qcopy(Tb):
            return dve0 + Tb * DV_TB + (9 if rope else 1)

        def dv_kready(Tb):
            return dv_sw(Tb, 15) if rope else dve0 + Tb * DV_TB + 2

        def dv_last(Tb):
            return dve0 + (Tb + 1) * DV_TB

        def loads_body(eng):
            for w, srcname in ((wq, "wq"), (wk, "wk"), (wv, "wv")):
                P.inc("qg", eng.dma_start(
                    out=w[:], in_=acfg[srcname].rearrange("(k q) n -> q k n", q=128)), 16)

        def sync_body(eng):
            for Tb in range(4):
                if rope:
                    if Tb >= 2:
                        P.wait(eng, "dve", dv_last(Tb - 2))
                    P.inc("qs", eng.dma_start(
                        out=cosb[:, Tb % 2, :],
                        in_=acfg["cosT"][:, Tb * 512:(Tb + 1) * 512]), 16)
                    P.inc("qs", eng.dma_start(
                        out=sinb[:, Tb % 2, :],
                        in_=acfg["sinT"][:, Tb * 512:(Tb + 1) * 512]), 16)
                P.wait(eng, "dve", dv_kready(Tb))
                ksrc = rotk if rope else kst
                P.inc("qs", eng.dma_start(
                    out=acfg["kt_own"][:, Tb * 512:(Tb + 1) * 512]
                    .rearrange("(c q) t -> q c t", q=128),
                    in_=ksrc[:]), 16)
                P.wait(eng, "dve", dv_last(Tb))
                P.inc("qs", eng.dma_start(
                    out=acfg["v_own"][Tb * 512:(Tb + 1) * 512, :]
                    .rearrange("(t q) f -> q t f", q=128),
                    in_=vst[:]), 16)

        def te_body(eng):
            P.wait(eng, "qg", qg0 + 48)
            for Tb in range(4):
                for sec, (n, wsb) in enumerate((("q", wq), ("k", wk))):
                    for c in range(NCH):
                        m = Tb * 16 + sec * 8 + c
                        if m >= 2:
                            pm = m - 2
                            pTb, prem = divmod(pm, 16)
                            psec, pc = divmod(prem, 8)
                            P.wait(eng, "act",
                                   ev_q(pTb, pc) if psec == 0 else ev_k(pTb, pc))
                        for k in range(NCH):
                            ins = eng.matmul(psQ[:, c % 2, :],
                                             wsb[:, k, c * 128:(c + 1) * 128],
                                             actT[:, k, Tb * 512:(Tb + 1) * 512],
                                             start=(k == 0), stop=(k == NCH - 1))
                        P.inc("pe", ins)
                for tl in range(4):
                    m = Tb * 4 + tl
                    if m >= 2:
                        pTb, ptl = divmod(m - 2, 4)
                        P.wait(eng, "act", ev_v(pTb, ptl))
                    for oc in range(2):
                        for k in range(NCH):
                            ins = eng.matmul(
                                psV[:, tl % 2, oc * 512:(oc + 1) * 512],
                                actT[:, k, (Tb * 4 + tl) * TT:(Tb * 4 + tl + 1) * TT],
                                wv[:, k, oc * 512:(oc + 1) * 512],
                                start=(k == 0), stop=(k == NCH - 1))
                    P.inc("pe", ins)
                if rope:
                    for sec, srcb in ((0, qst), (1, kst)):
                        for c in range(NCH):
                            sw_i = sec * 8 + c
                            gsw = Tb * 16 + sw_i
                            if gsw >= 2:
                                pTb, psw = divmod(gsw - 2, 16)
                                P.wait(eng, "dve", dv_sw(pTb, psw))
                            P.wait(eng, "act",
                                   ev_q(Tb, c) if sec == 0 else ev_k(Tb, c))
                            ins = eng.matmul(psSw[:, sw_i % 2, :], C["swapb"][:],
                                             srcb[:, c, :], start=True, stop=True)
                            P.inc("pe", ins)

        def sc_body(eng):
            for Tb in range(4):
                if Tb > 0:
                    P.wait(eng, "qs", qs_at(Tb - 1))
                    P.wait(eng, "dve", dv_last(Tb - 1))
                for c in range(NCH):
                    P.wait(eng, "pe", pe0 + Tb * PE_TB + c + 1)
                    P.inc("act", eng.activation(out=qst[:, c, :],
                                                in_=psQ[:, c % 2, :], func=AF.Copy))
                for c in range(NCH):
                    P.wait(eng, "pe", pe0 + Tb * PE_TB + 8 + c + 1)
                    P.inc("act", eng.activation(out=kst[:, c, :],
                                                in_=psQ[:, c % 2, :], func=AF.Copy))
                for tl in range(4):
                    P.wait(eng, "pe", pe0 + Tb * PE_TB + 16 + tl + 1)
                    P.inc("act", eng.activation(out=vst[:, tl, :],
                                                in_=psV[:, tl % 2, :], func=AF.Copy))

        def ve_body(eng):
            for Tb in range(4):
                if rope:
                    cq = cosb[:, Tb % 2, :].unsqueeze(1).broadcast_to([128, NCH, 512])
                    sq = sinb[:, Tb % 2, :]
                    P.wait(eng, "qs",
                           qs0 + sum(n_qs_tb(t) for t in range(Tb)) * 16 + 32)
                    for sec, (srcb, dstb) in ((0, (qst, rotq)), (1, (kst, rotk))):
                        P.wait(eng, "act",
                               ev_q(Tb, NCH - 1) if sec == 0 else ev_k(Tb, NCH - 1))
                        eng.tensor_tensor(out=tsc[:], in0=srcb[:], in1=cq,
                                          op=ALU.mult)
                        for c in range(NCH):
                            sw_i = sec * 8 + c
                            P.wait(eng, "pe", pe_sw(Tb, sw_i))
                            eng.tensor_tensor(out=dstb[:, c, :],
                                              in0=psSw[:, sw_i % 2, :],
                                              in1=sq, op=ALU.mult)
                            ins = eng.tensor_add(out=dstb[:, c, :],
                                                 in0=dstb[:, c, :],
                                                 in1=tsc[:, c, :])
                            P.inc("dve", ins)
                        if sec == 0:
                            P.wait(eng, "pe", pe0 + Tb * PE_TB + 20)
                            ins = eng.tensor_copy(
                                out=actT[:, :, Tb * 512:(Tb + 1) * 512],
                                in_=rotq[:])
                            P.inc("dve", ins)
                    P.wait(eng, "act", ev_v(Tb, 3))
                    P.inc("dve", eng.nop())
                else:
                    P.wait(eng, "act", ev_q(Tb, NCH - 1))
                    P.wait(eng, "pe", pe0 + (Tb + 1) * PE_TB)
                    ins = eng.tensor_copy(
                        out=actT[:, :, Tb * 512:(Tb + 1) * 512], in_=qst[:])
                    P.inc("dve", ins)
                    P.wait(eng, "act", ev_k(Tb, NCH - 1))
                    P.inc("dve", eng.nop())
                    P.wait(eng, "act", ev_v(Tb, 3))
                    P.inc("dve", eng.nop())

        run_phase(P, block, {"gpsimd": loads_body, "sync": sync_body,
                             "tensor": te_body, "scalar": sc_body, "vector": ve_body})


def _qh(buf, h, tok, DH, ping=None):
    c = h // 2 if DH == 64 else h
    off = 64 * (h % 2) if DH == 64 else 0
    if ping is None:
        return buf[off:off + DH, c, tok]
    return buf[off:off + DH, ping, c, tok]


def ph_pass1(P, block, C, acfg):
    nc = P.nc
    H = acfg["H"]
    DH = DM // H
    HB = 4
    NHB = H // HB
    with ExitStack() as ph:
        ktb = [ph.enter_context(nc.sbuf_tensor(_nm(f"p1_kt{j}"), [128, NCH, 512], BF16))
               for j in range(2)]
        mall = ph.enter_context(nc.sbuf_tensor(_nm("p1_m"), [128, H], F32))
        msc = ph.enter_context(nc.sbuf_tensor(_nm("p1_ms"), [128, 4], F32))
        negm = ph.enter_context(nc.sbuf_tensor(_nm("p1_nm"), [1, 4], F32))
        psS = ph.enter_context(nc.psum_tensor(_nm("p1_psS"), [128, HB, 512], F32))
        psM = ph.enter_context(nc.psum_tensor(_nm("p1_psM"), [1, 128], F32))
        psB = ph.enter_context(nc.psum_tensor(_nm("p1_psB"), [128, 1], F32))
        actT = C["actT"]
        pe0, dve0, qs0 = P.val("pe"), P.val("dve"), P.val("qs")

        def nld(i):
            return 1 if i == 0 else 2

        pe_at = {}
        dve_at = {}
        qs_at = {}
        pn = dn = qn = 0
        for i in range(NT):
            for g2 in range(nld(i)):
                qn += 1
                qs_at[("ld", i, g2)] = qn
                for hb in range(NHB):
                    pn += 1
                    pe_at[("S", i, g2, hb)] = pn
                    dn += 1
                    dve_at[("rm", i, g2, hb)] = dn
            dn += 1
            dve_at[("hred", i)] = dn
            pn += 1
            pe_at[("tr", i)] = pn
            dn += 1
            dve_at[("negm", i)] = dn
            pn += 1
            pe_at[("bc", i)] = pn
            dn += 1
            dve_at[("mbc", i)] = dn

        units = [(i, g2) for i in range(NT) for g2 in range(nld(i))]

        def sync_body(eng):
            for un, (i, g2) in enumerate(units):
                g = max(0, i - 1) + g2
                if un >= 2:
                    pi, pg = units[un - 2]
                    P.wait(eng, "pe", pe0 + pe_at[("S", pi, pg, NHB - 1)])
                P.inc("qs", eng.dma_start(
                    out=ktb[un % 2][:],
                    in_=acfg["kt_full"][:, g * 512:(g + 1) * 512]
                    .rearrange("(c q) t -> q c t", q=128)), 16)

        def te_body(eng):
            for un, (i, g2) in enumerate(units):
                P.wait(eng, "qs", qs0 + qs_at[("ld", i, g2)] * 16)
                for hb in range(NHB):
                    gl = un * NHB + hb
                    if gl >= 1:
                        pi, pg = units[(gl - 1) // NHB]
                        phb = (gl - 1) % NHB
                        P.wait(eng, "dve", dve0 + dve_at[("rm", pi, pg, phb)])
                    for hh in range(HB):
                        h = hb * HB + hh
                        ins = eng.matmul(psS[:, hh, :],
                                         _qh(actT, h, slice(i * TT, (i + 1) * TT), DH),
                                         _qh(ktb[un % 2], h, slice(0, 512), DH),
                                         start=True, stop=True)
                    P.inc("pe", ins)
                if g2 == nld(i) - 1:
                    P.wait(eng, "dve", dve0 + dve_at[("hred", i)])
                    if i > 0:
                        P.wait(eng, "dve", dve0 + dve_at[("mbc", i - 1)])
                    ins = eng.transpose(psM[:], msc[:, 0:1], C["identf"][:])
                    P.inc("pe", ins)
                    P.wait(eng, "dve", dve0 + dve_at[("negm", i)])
                    ins = eng.matmul(psB[:], C["onesf"][0:1, :],
                                     negm[0:1, i % 4:i % 4 + 1], start=True, stop=True)
                    P.inc("pe", ins)

        def ve_body(eng):
            for i in range(NT):
                eng.memset(mall[:], -3.0e38)
                for g2 in range(nld(i)):
                    un = units.index((i, g2))
                    for hb in range(NHB):
                        P.wait(eng, "pe", pe0 + pe_at[("S", i, g2, hb)])
                        P.edge(eng, eng.tensor_reduce(
                            out=msc[:].rearrange("p (a b) -> p a b", b=1),
                            in_=psS[:], axis=AX.X, op=ALU.max))
                        ins = eng.tensor_max(
                            out=mall[:, hb * HB:(hb + 1) * HB],
                            in0=mall[:, hb * HB:(hb + 1) * HB],
                            in1=msc[:])
                        P.inc("dve", ins)
                        P.wait(eng, "dve", dve0 + dve_at[("rm", i, g2, hb)])
                ins = eng.tensor_reduce(out=msc[:, 0:1], in_=mall[:], axis=AX.X,
                                        op=ALU.max)
                P.inc("dve", ins)
                P.wait(eng, "pe", pe0 + pe_at[("tr", i)])
                ins = eng.tensor_reduce(out=negm[0:1, i % 4:i % 4 + 1], in_=psM[:],
                                        axis=AX.X, op=ALU.max, negate=True)
                P.inc("dve", ins)
                P.wait(eng, "pe", pe0 + pe_at[("bc", i)])
                ins = eng.tensor_scalar(out=C["negMbc"][:, i:i + 1], in0=psB[:],
                                        scalar1=acfg["scale"], scalar2=None,
                                        op0=ALU.mult)
                P.inc("dve", ins)

        run_phase(P, block, {"sync": sync_body, "tensor": te_body, "vector": ve_body})


def ph_pass2(P, block, C, acfg):
    nc = P.nc
    H = acfg["H"]
    DH = DM // H
    HB = 8
    NHB = H // HB
    VW = DH + 2
    HPB = 512 // VW
    NBK = (H + HPB - 1) // HPB
    with ExitStack() as ph:
        ktb = [ph.enter_context(nc.sbuf_tensor(_nm(f"p2_kt{j}"), [128, NCH, 512], BF16))
               for j in range(2)]
        vb = [ph.enter_context(nc.sbuf_tensor(_nm(f"p2_v{j}"), [128, 4, H * VW], BF16))
              for j in range(2)]
        pb = ph.enter_context(nc.sbuf_tensor(_nm("p2_pb"), [128, HB * TT], BF16))
        mkb = ph.enter_context(nc.sbuf_tensor(_nm("p2_mk"), [128, 2, 4, TT], BF16))
        ofin = ph.enter_context(nc.sbuf_tensor(_nm("p2_of"), [128, NBK * HPB * DH], F32))
        obf = ph.enter_context(nc.sbuf_tensor(_nm("p2_ob"), [128, DM], BF16))
        oT = ph.enter_context(nc.sbuf_tensor(_nm("p2_oT"), [128, NCH, TT], BF16))
        lrec = ph.enter_context(nc.sbuf_tensor(_nm("p2_lr"), [128, NBK * HPB], F32))
        wo = ph.enter_context(nc.sbuf_tensor(_nm("p2_wo"), [128, NCH, DM], BF16))
        psST = ph.enter_context(nc.psum_tensor(_nm("p2_psST"), [128, HB * TT], F32))
        psO = ph.enter_context(nc.psum_tensor(_nm("p2_psO"), [128, NBK, 512], F32))
        psT = ph.enter_context(nc.psum_tensor(_nm("p2_psT"), [128, NCH, TT], BF16))
        psR = ph.enter_context(nc.psum_tensor(_nm("p2_psR"), [128, DM], F32))
        actT, x = C["actT"], C["x"]
        pe0, act0, dve0, qs0, qg0 = (P.val("pe"), P.val("act"), P.val("dve"),
                                     P.val("qs"), P.val("qg"))

        def obank(h):
            return (h // HPB, (h % HPB) * VW)

        def stcol(hh):
            return (hh % 2) * (HB // 2) + hh // 2 if DH == 64 else hh

        pe_at, act_at, dve_at, qs_at = {}, {}, {}, {}
        pn = an = dn = qn = 0
        kt_units = []
        dn += 1
        dve_at["junkset"] = dn
        for i in range(NT):
            qn += 1
            qs_at[("mk", i)] = qn
            for g in range(i + 1):
                qn += 2
                qs_at[("kv", i, g)] = qn
                for j in range(4):
                    for hb in range(NHB):
                        kt_units.append((i, g, j, hb))
                        pn += 1
                        pe_at[("st", i, g, j, hb)] = pn
                        an += 1
                        act_at[("exp", i, g, j, hb)] = an
                        if g == i:
                            dn += 1
                            dve_at[("msk", i, g, j, hb)] = dn
                        pn += 1
                        pe_at[("av", i, g, j, hb)] = pn
            dn += 1
            dve_at[("lrec", i)] = dn
            dn += 1
            dve_at[("omul", i)] = dn
            an += 1
            act_at[("obf", i)] = an
            pn += 1
            pe_at[("tr", i)] = pn
            dn += 1
            dve_at[("oT", i)] = dn
            pn += 1
            pe_at[("proj", i)] = pn
            dn += 1
            dve_at[("xadd", i)] = dn

        kv_units = [(i, g) for i in range(NT) for g in range(i + 1)]

        def sync_body(eng):
            for i in range(NT):
                if i >= 2 and i - 2 >= 0:
                    P.wait(eng, "dve", dve0 + dve_at[("msk", i - 2, i - 2, 3, NHB - 1)])
                P.inc("qs", eng.dma_start(
                    out=mkb[:, i % 2, :, :],
                    in_=acfg["maskT"][i].rearrange("j k q -> k j q")), 16)
                for g in range(i + 1):
                    un = kv_units.index((i, g))
                    if un >= 2:
                        pi, pg = kv_units[un - 2]
                        P.wait(eng, "pe", pe0 + pe_at[("av", pi, pg, 3, NHB - 1)])
                    P.inc("qs", eng.dma_start(
                        out=ktb[un % 2][:],
                        in_=acfg["kt_full"][:, g * 512:(g + 1) * 512]
                        .rearrange("(c q) t -> q c t", q=128)), 16)
                    P.inc("qs", eng.dma_start(
                        out=vb[un % 2][:],
                        in_=acfg["v_full"][g * 512:(g + 1) * 512, :]
                        .rearrange("(t q) f -> q t f", q=128)), 16)

        def gp_body(eng):
            P.inc("qg", eng.dma_start(
                out=wo[:], in_=acfg["wo"].rearrange("(k q) n -> q k n", q=128)), 16)

        def te_body(eng):
            P.wait(eng, "qg", qg0 + 16)
            for i in range(NT):
                for g in range(i + 1):
                    un = kv_units.index((i, g))
                    P.wait(eng, "qs", qs0 + qs_at[("kv", i, g)] * 16)
                    for j in range(4):
                        for hb in range(NHB):
                            ki = kt_units.index((i, g, j, hb))
                            if ki >= 1:
                                pk = kt_units[ki - 1]
                                P.wait(eng, "act", act0 + act_at[("exp",) + pk])
                            for hh in range(HB):
                                h = hb * HB + hh
                                cs = stcol(hh)
                                ins = eng.matmul(psST[:, cs * TT:(cs + 1) * TT],
                                                 _qh(ktb[un % 2], h, slice(j * TT, (j + 1) * TT), DH),
                                                 _qh(actT, h, slice(i * TT, (i + 1) * TT), DH),
                                                 start=True, stop=True)
                            P.inc("pe", ins)
                            P.wait(eng, "act", act0 + act_at[("exp", i, g, j, hb)])
                            if g == i:
                                P.wait(eng, "dve", dve0 + dve_at[("msk", i, g, j, hb)])
                            if i == 0 and g == 0 and j == 0 and hb == 0:
                                P.wait(eng, "dve", dve0 + dve_at["junkset"])
                            if i > 0 and g == 0 and j == 0 and hb == 0:
                                P.wait(eng, "dve", dve0 + dve_at[("omul", i - 1)])
                            for hh in range(HB):
                                h = hb * HB + hh
                                bk, off = obank(h)
                                cs = stcol(hh)
                                ins = eng.matmul(psO[:, bk, off:off + VW],
                                                 pb[:, cs * TT:(cs + 1) * TT],
                                                 vb[un % 2][:, j, h * VW:(h + 1) * VW],
                                                 start=(g == 0 and j == 0 and h % HPB == 0),
                                                 stop=(g == i and j == 3 and
                                                       (h % HPB == HPB - 1 or h == H - 1)))
                            P.inc("pe", ins)
                P.wait(eng, "act", act0 + act_at[("obf", i)])
                if i > 0:
                    P.wait(eng, "dve", dve0 + dve_at[("oT", i - 1)])
                for c in range(NCH):
                    ins = eng.transpose(psT[:, c, :], obf[:, c * TT:(c + 1) * TT],
                                        C["identb"][:])
                P.inc("pe", ins)
                P.wait(eng, "dve", dve0 + dve_at[("oT", i)])
                if i > 0:
                    P.wait(eng, "dve", dve0 + dve_at[("xadd", i - 1)])
                for oc in range(2):
                    first = True
                    if acfg["bo"] is not None:
                        eng.matmul(psR[:, oc * 512:(oc + 1) * 512], C["onesf"][0:1, :],
                                   acfg["bo"][0:1, oc * 512:(oc + 1) * 512],
                                   start=True, stop=False)
                        first = False
                    for k in range(NCH):
                        ins = eng.matmul(psR[:, oc * 512:(oc + 1) * 512], oT[:, k, :],
                                         wo[:, k, oc * 512:(oc + 1) * 512],
                                         start=(first and k == 0), stop=(k == NCH - 1))
                P.inc("pe", ins)

        def sc_body(eng):
            for i in range(NT):
                for g in range(i + 1):
                    for j in range(4):
                        for hb in range(NHB):
                            ki = kt_units.index((i, g, j, hb))
                            P.wait(eng, "pe", pe0 + pe_at[("st", i, g, j, hb)])
                            if ki >= 1:
                                pk = kt_units[ki - 1]
                                P.wait(eng, "pe", pe0 + pe_at[("av",) + pk])
                            P.inc("act", eng.activation(
                                out=pb[:], in_=psST[:], func=AF.Exp,
                                scale=acfg["scale"],
                                bias=C["negMbc"][:, i:i + 1]))
                P.wait(eng, "dve", dve0 + dve_at[("omul", i)])
                if i > 0:
                    P.wait(eng, "pe", pe0 + pe_at[("tr", i - 1)])
                P.inc("act", eng.activation(out=obf[:], in_=ofin[:, 0:DM],
                                            func=AF.Copy))

        def ve_body(eng):
            P.inc("dve", eng.memset(psO[:], 1.0))
            for i in range(NT):
                for j in range(4):
                    for hb in range(NHB):
                        g = i
                        P.wait(eng, "act", act0 + act_at[("exp", i, g, j, hb)])
                        P.wait(eng, "qs", qs0 + qs_at[("mk", i)] * 16)
                        ins = eng.tensor_tensor(
                            out=pb[:].rearrange("p (h q) -> p h q", h=HB),
                            in0=pb[:].rearrange("p (h q) -> p h q", h=HB),
                            in1=mkb[:, i % 2, j, :].unsqueeze(1)
                            .broadcast_to([128, HB, TT]),
                            op=ALU.mult)
                        P.inc("dve", ins)
                P.wait(eng, "pe", pe0 + pe_at[("av", i, i, 3, NHB - 1)])
                P.edge(eng, eng.tensor_copy(
                    out=lrec[:].rearrange("p (b s) -> p b s", b=NBK),
                    in_=psO[:, :, 0:HPB * VW].rearrange("p b (s w) -> p b s w", w=VW)[:, :, :, 0]))
                ins = eng.reciprocal(out=lrec[:, 0:H], in_=lrec[:, 0:H])
                P.inc("dve", ins)
                P.wait(eng, "dve", dve0 + dve_at[("lrec", i)])
                ins = eng.tensor_tensor(
                    out=ofin[:].rearrange("p (b s d) -> p b s d", b=NBK, s=HPB),
                    in0=psO[:, :, 0:HPB * VW].rearrange("p b (s w) -> p b s w", w=VW)[:, :, :, 1:DH + 1],
                    in1=lrec[:].rearrange("p (b s) -> p b s", b=NBK).unsqueeze(3)
                    .broadcast_to([128, NBK, HPB, DH]),
                    op=ALU.mult)
                P.inc("dve", ins)
                P.wait(eng, "pe", pe0 + pe_at[("tr", i)])
                P.inc("dve", eng.tensor_copy(out=oT[:], in_=psT[:]))
                P.wait(eng, "pe", pe0 + pe_at[("proj", i)])
                ins = eng.scalar_tensor_tensor(out=x[:, i, :], in0=psR[:], scalar=1.0,
                                               in1=x[:, i, :], op0=ALU.mult,
                                               op1=ALU.add)
                P.inc("dve", ins)

        run_phase(P, block, {"sync": sync_body, "gpsimd": gp_body, "tensor": te_body,
                             "scalar": sc_body, "vector": ve_body})



def _common_alloc(nc, stack):
    C = {}
    C["x"] = stack.enter_context(nc.sbuf_tensor(_nm("x_res"), [128, NT, DM], F32))[:]
    C["actT"] = stack.enter_context(nc.sbuf_tensor(_nm("actT"), [128, NCH, NTOK], BF16))[:]
    C["identb"] = stack.enter_context(nc.sbuf_tensor(_nm("identb"), [128, 128], BF16))
    C["swapb"] = stack.enter_context(nc.sbuf_tensor(_nm("swapb"), [128, 128], BF16))[:]
    C["identf"] = stack.enter_context(nc.sbuf_tensor(_nm("identf"), [128, 128], F32))
    C["onesf"] = stack.enter_context(nc.sbuf_tensor(_nm("onesf"), [1, 128], F32))
    C["iotarev"] = stack.enter_context(nc.sbuf_tensor(_nm("iotarev"), [128, NEXP], F32))
    C["negMbc"] = stack.enter_context(nc.sbuf_tensor(_nm("negMbc"), [128, NT], F32))[:]
    return C


def _const_inputs(nc):
    d = {}
    d["c_identb"] = nc.dram_tensor("c_identb", [128, 128], BF16, kind="ExternalInput").ap()
    d["c_swapb"] = nc.dram_tensor("c_swapb", [128, 128], BF16, kind="ExternalInput").ap()
    d["c_identf"] = nc.dram_tensor("c_identf", [128, 128], F32, kind="ExternalInput").ap()
    d["c_onesf"] = nc.dram_tensor("c_onesf", [1, 128], F32, kind="ExternalInput").ap()
    d["c_iota"] = nc.dram_tensor("c_iota", [128, NEXP], F32, kind="ExternalInput").ap()
    return d


def _const_host():
    return {
        "c_identb": np.eye(128, dtype=bf16),
        "c_swapb": np.roll(np.eye(128, dtype=bf16), 64, axis=1),
        "c_identf": np.eye(128, dtype=np.float32),
        "c_onesf": np.ones((1, 128), dtype=np.float32),
        "c_iota": np.tile((8.0 * (7 - np.arange(NEXP, dtype=np.float32)))[None, :],
                          (128, 1)),
    }


def _const_loads(C, d):
    return [
        (C["identb"][:], d["c_identb"], "qs"),
        (C["swapb"], d["c_swapb"], "qs"),
        (C["identf"][:], d["c_identf"], "qs"),
        (C["onesf"][:], d["c_onesf"], "qs"),
        (C["iotarev"][:], d["c_iota"], "qs"),
    ]


def build_launch1():
    nc = bass.Bass("TRN2", target_bir_lowering=False, detect_race_conditions=False)
    io = _const_inputs(nc)
    for n, shp, dt in [
        ("x_in", [NTOK, DM], F32), ("fc1_q", [DM, IF], I32), ("fc1_s", [8, IF], F32),
        ("fc2_q", [IF, DM], I32), ("fc2_s", [32, DM], F32), ("g0", [128, NCH], F32),
        ("be0", [128, NCH], F32), ("b1t0", [128, 32], F32), ("b2r0", [1, DM], F32),
        ("wq0", [DM, DM], F32), ("wk0", [DM, DM], F32), ("wv0", [DM, DM], F32),
    ]:
        io[n] = nc.dram_tensor(n, shp, dt, kind="ExternalInput").ap()
    for n, shp, dt in [
        ("wdeq1", [DM, IF], BF16), ("wdeq2", [IF, DM], BF16),
        ("x2_out", [NTOK, DM], F32), ("kt0", [DM, NTOK], BF16),
        ("v0", [NTOK, DM], BF16), ("qt0", [DM, NTOK], BF16),
    ]:
        io[n] = nc.dram_tensor(n, shp, dt, kind="ExternalOutput").ap()

    with ExitStack() as stack:
        C = _common_alloc(nc, stack)
        g0 = stack.enter_context(nc.sbuf_tensor(_nm("g0s"), [128, NCH], F32))
        be0 = stack.enter_context(nc.sbuf_tensor(_nm("be0s"), [128, NCH], F32))
        b1t = stack.enter_context(nc.sbuf_tensor(_nm("b1t0s"), [128, 32], F32))
        block = stack.enter_context(nc.Block())
        P = Prog(nc, stack)
        loads = _const_loads(C, io) + [
            (g0[:], io["g0"], "qs"), (be0[:], io["be0"], "qs"),
            (b1t[:], io["b1t0"], "qs"),
        ]
        ph_load_const(P, block, loads)
        ph_load_x(P, block, C, io["x_in"])
        ph_dequant(P, block, C, io["fc1_q"], io["fc1_s"], io["wdeq1"], 8, IF)
        ph_dequant(P, block, C, io["fc2_q"], io["fc2_s"], io["wdeq2"], 32, DM)
        ph_prep(P, block, C, "ffn0")
        ph_fc(P, block, C, dict(
            nexp=1, act_fn=AF.Silu, castload=False,
            w1src=lambda e, p: io["wdeq1"][:, p * PIECE:(p + 1) * PIECE],
            w2src=lambda e, p: io["wdeq2"][p * PIECE:(p + 1) * PIECE, :],
            gam=lambda e: g0[:], bet=lambda e: be0[:],
            b1=lambda e: b1t[:], b2src=lambda e: io["b2r0"],
            comb=("const", 0.25)))
        ph_prep(P, block, C, "plain")
        ph_qkv(P, block, C, dict(
            rope=False, wq=io["wq0"], wk=io["wk0"], wv=io["wv0"],
            kt_own=io["kt0"], v_own=io["v0"], cosT=None, sinT=None))
        ph_store_x(P, block, C, io["x2_out"])

        def qt_store(eng):
            P.inc("qs", eng.dma_start(
                out=io["qt0"].rearrange("(c q) t -> q c t", q=128), in_=C["actT"]), 16)
        run_phase(P, block, {"sync": qt_store})
    return nc


def _moe_io(nc, io):
    for n, shp, dt in [
        ("rngb", [128, NCH], F32), ("rnbb", [128, NCH], F32),
        ("wrt", [DM, NEXP], F32), ("rb", [1, NEXP], F32),
        ("lng", [NEXP, 128, NCH], F32), ("lnb", [NEXP, 128, NCH], F32),
        ("w1", [NEXP, DM, IF], F32), ("b1T", [NEXP, 128, 32], F32),
        ("w2", [NEXP, IF, DM], F32), ("b2", [NEXP, DM], F32),
    ]:
        io[n] = nc.dram_tensor(n, shp, dt, kind="ExternalInput").ap()


def _moe_alloc(nc, stack):
    d = {}
    d["rng"] = stack.enter_context(nc.sbuf_tensor(_nm("rngs"), [128, NCH], F32))
    d["rnb"] = stack.enter_context(nc.sbuf_tensor(_nm("rnbs"), [128, NCH], F32))
    d["wrt"] = stack.enter_context(nc.sbuf_tensor(_nm("wrts"), [128, NCH, NEXP], BF16))
    d["rbs"] = stack.enter_context(nc.sbuf_tensor(_nm("rbs"), [1, NEXP], F32))
    d["lng"] = stack.enter_context(nc.sbuf_tensor(_nm("lngs"), [128, NEXP, NCH], F32))
    d["lnb"] = stack.enter_context(nc.sbuf_tensor(_nm("lnbs"), [128, NEXP, NCH], F32))
    d["b1T"] = stack.enter_context(nc.sbuf_tensor(_nm("b1Ts"), [128, NEXP, 32], F32))
    d["cmb"] = stack.enter_context(nc.sbuf_tensor(_nm("cmbs"), [128, NT * NEXP], F32))
    return d


def _moe_loads(m, io):
    return [
        (m["rng"][:], io["rngb"], "qs"), (m["rnb"][:], io["rnbb"], "qs"),
        (m["rbs"][:], io["rb"], "qs"),
        (m["lng"][:], io["lng"].rearrange("e q c -> q e c"), "qs"),
        (m["lnb"][:], io["lnb"].rearrange("e q c -> q e c"), "qs"),
        (m["b1T"][:], io["b1T"].rearrange("e q c -> q e c"), "qs"),
        (m["wrt"][:], io["wrt"].rearrange("(c q) e -> q c e", q=128), "qg"),
    ]


def _moe_fc_cfg(m, io):
    return dict(
        nexp=NEXP, act_fn=AF.Gelu, castload=True,
        w1src=lambda e, p: io["w1"][e, :, p * PIECE:(p + 1) * PIECE],
        w2src=lambda e, p: io["w2"][e, p * PIECE:(p + 1) * PIECE, :],
        gam=lambda e: m["lng"][:, e, :], bet=lambda e: m["lnb"][:, e, :],
        b1=lambda e: m["b1T"][:, e, :],
        b2src=lambda e: io["b2"][e:e + 1, :],
        comb=("cmb", m["cmb"][:]))


def build_launch2():
    nc = bass.Bass("TRN2", target_bir_lowering=False, detect_race_conditions=False)
    io = _const_inputs(nc)
    for n, shp, dt in [
        ("x_in", [NTOK, DM], F32), ("qt_in", [DM, NTOK], BF16),
        ("ktf", [DM, SEQ], BF16), ("vf", [SEQ, 16 * 66], BF16),
        ("maskT", [NT, 4, 128, 128], BF16), ("wo0", [DM, DM], F32),
        ("bo0", [1, DM], F32), ("wdeq1", [DM, IF], BF16), ("wdeq2", [IF, DM], BF16),
        ("g0", [128, NCH], F32), ("be0", [128, NCH], F32), ("b1t0", [128, 32], F32),
        ("b2r0", [1, DM], F32), ("wq1", [DM, DM], F32), ("wk1", [DM, DM], F32),
        ("wv1", [DM, DM], F32), ("cosT", [128, NTOK], BF16), ("sinT", [128, NTOK], BF16),
    ]:
        io[n] = nc.dram_tensor(n, shp, dt, kind="ExternalInput").ap()
    _moe_io(nc, io)
    for n, shp, dt in [
        ("x5_out", [NTOK, DM], F32), ("kt1", [DM, NTOK], BF16),
        ("v1", [NTOK, DM], BF16), ("qt1", [DM, NTOK], BF16),
    ]:
        io[n] = nc.dram_tensor(n, shp, dt, kind="ExternalOutput").ap()

    with ExitStack() as stack:
        C = _common_alloc(nc, stack)
        g0 = stack.enter_context(nc.sbuf_tensor(_nm("g0s"), [128, NCH], F32))
        be0 = stack.enter_context(nc.sbuf_tensor(_nm("be0s"), [128, NCH], F32))
        b1t = stack.enter_context(nc.sbuf_tensor(_nm("b1t0s"), [128, 32], F32))
        bo_sb = stack.enter_context(nc.sbuf_tensor(_nm("bo0s"), [1, DM], F32))
        m = _moe_alloc(nc, stack)
        block = stack.enter_context(nc.Block())
        P = Prog(nc, stack)
        loads = _const_loads(C, io) + [
            (g0[:], io["g0"], "qs"), (be0[:], io["be0"], "qs"),
            (b1t[:], io["b1t0"], "qs"), (bo_sb[:], io["bo0"], "qs"),
        ] + _moe_loads(m, io)
        ph_load_const(P, block, loads)
        ph_load_x(P, block, C, io["x_in"])

        def qt_load(eng):
            P.inc("qs", eng.dma_start(
                out=C["actT"], in_=io["qt_in"].rearrange("(c q) t -> q c t", q=128)), 16)
        run_phase(P, block, {"sync": qt_load})

        acfg0 = dict(H=16, scale=1.0, rope=False, kt_full=io["ktf"], v_full=io["vf"],
                     maskT=io["maskT"], wo=io["wo0"], bo=bo_sb[:])
        ph_pass1(P, block, C, acfg0)
        ph_pass2(P, block, C, acfg0)
        ph_prep(P, block, C, "ffn0")
        ph_fc(P, block, C, dict(
            nexp=1, act_fn=AF.Silu, castload=False,
            w1src=lambda e, p: io["wdeq1"][:, p * PIECE:(p + 1) * PIECE],
            w2src=lambda e, p: io["wdeq2"][p * PIECE:(p + 1) * PIECE, :],
            gam=lambda e: g0[:], bet=lambda e: be0[:],
            b1=lambda e: b1t[:], b2src=lambda e: io["b2r0"],
            comb=("const", 0.25)))
        ph_prep(P, block, C, "plain")
        ph_router(P, block, C, dict(rng=m["rng"][:], rnb=m["rnb"][:], wrt=m["wrt"][:],
                                    rb=m["rbs"][:], cmb=m["cmb"][:]))
        ph_fc(P, block, C, _moe_fc_cfg(m, io))
        ph_prep(P, block, C, "plain")
        ph_qkv(P, block, C, dict(
            rope=True, wq=io["wq1"], wk=io["wk1"], wv=io["wv1"],
            kt_own=io["kt1"], v_own=io["v1"], cosT=io["cosT"], sinT=io["sinT"]))
        ph_store_x(P, block, C, io["x5_out"])

        def qt_store(eng):
            P.inc("qs", eng.dma_start(
                out=io["qt1"].rearrange("(c q) t -> q c t", q=128), in_=C["actT"]), 16)
        run_phase(P, block, {"sync": qt_store})
    return nc


def build_launch3():
    nc = bass.Bass("TRN2", target_bir_lowering=False, detect_race_conditions=False)
    io = _const_inputs(nc)
    for n, shp, dt in [
        ("x_in", [NTOK, DM], F32), ("qt_in", [DM, NTOK], BF16),
        ("ktf", [DM, SEQ], BF16), ("vf", [SEQ, 8 * 130], BF16),
        ("maskT", [NT, 4, 128, 128], BF16), ("wo1", [DM, DM], F32),
    ]:
        io[n] = nc.dram_tensor(n, shp, dt, kind="ExternalInput").ap()
    _moe_io(nc, io)
    io["x_out"] = nc.dram_tensor("x_out", [NTOK, DM], F32, kind="ExternalOutput").ap()

    with ExitStack() as stack:
        C = _common_alloc(nc, stack)
        m = _moe_alloc(nc, stack)
        block = stack.enter_context(nc.Block())
        P = Prog(nc, stack)
        loads = _const_loads(C, io) + _moe_loads(m, io)
        ph_load_const(P, block, loads)
        ph_load_x(P, block, C, io["x_in"])

        def qt_load(eng):
            P.inc("qs", eng.dma_start(
                out=C["actT"], in_=io["qt_in"].rearrange("(c q) t -> q c t", q=128)), 16)
        run_phase(P, block, {"sync": qt_load})

        acfg1 = dict(H=8, scale=1.0 / 128.0, rope=True, kt_full=io["ktf"],
                     v_full=io["vf"], maskT=io["maskT"], wo=io["wo1"], bo=None)
        ph_pass1(P, block, C, acfg1)
        ph_pass2(P, block, C, acfg1)
        ph_prep(P, block, C, "plain")
        ph_router(P, block, C, dict(rng=m["rng"][:], rnb=m["rnb"][:], wrt=m["wrt"][:],
                                    rb=m["rbs"][:], cmb=m["cmb"][:]))
        ph_fc(P, block, C, _moe_fc_cfg(m, io))
        ph_store_x(P, block, C, io["x_out"])
    return nc



def _pp(v):
    v = np.asarray(v, np.float32)
    n = v.shape[0] // 128
    return np.ascontiguousarray(v.reshape(n, 128).T)


def _tiles_rows(core):
    b, jj = core // 4, core % 4
    idx = []
    for i in range(NT):
        t = jj + 4 * i
        idx.extend(range(b * SEQ + t * TT, b * SEQ + (t + 1) * TT))
    return np.array(idx)


def _build_masks(core):
    jj = core % 4
    m = np.zeros((NT, 4, 128, 128), dtype=bf16)
    kk = np.arange(128)[:, None]
    qq = np.arange(128)[None, :]
    for i in range(NT):
        for j in range(4):
            gk0 = (4 * i + j) * TT
            gq0 = (jj + 4 * i) * TT
            m[i, j] = ((gk0 + kk) <= (gq0 + qq)).astype(bf16)
    return m


def _build_rope(core):
    jj = core % 4
    pos = np.concatenate([np.arange((jj + 4 * i) * TT, (jj + 4 * i + 1) * TT)
                          for i in range(NT)]).astype(np.float64)
    d = np.arange(64, dtype=np.float64)
    freqs = np.power(10000.0, -d / 64.0)
    ang = pos[None, :] * freqs[:, None]
    cos = np.cos(ang).astype(np.float32)
    sin = np.sin(ang).astype(np.float32)
    C = np.concatenate([cos, cos], axis=0).astype(bf16)
    S = np.concatenate([-sin, sin], axis=0).astype(bf16)
    return np.ascontiguousarray(C), np.ascontiguousarray(S)


def _assemble_ktf(kt_owns, b):
    ktf = np.empty((DM, SEQ), dtype=bf16)
    kv = ktf.reshape(DM, 64, TT)
    for jj in range(4):
        kv[:, jj::4, :] = np.asarray(kt_owns[b * 4 + jj]).reshape(DM, NT, TT)
    return ktf


def _assemble_vf(v_owns, b, H):
    DH = DM // H
    vf = np.zeros((SEQ, H, DH + 2), dtype=bf16)
    vf[:, :, 0] = 1.0
    vv = vf.reshape(64, TT, H, DH + 2)
    for jj in range(4):
        vo = np.asarray(v_owns[b * 4 + jj]).reshape(NT, TT, H, DH)
        vv[jj::4, :, :, 1:DH + 1] = vo
    return np.ascontiguousarray(vf.reshape(SEQ, H * (DH + 2)))


def kernel(**inputs):
    cores = list(range(NCORE))
    consts = _const_host()

    nc1 = build_launch1()
    x_full = np.ascontiguousarray(
        np.asarray(inputs["L0_attn_hidden_states"], np.float32).reshape(B * SEQ, DM))
    shared1 = dict(consts)
    shared1.update({
        "fc1_q": np.ascontiguousarray(np.asarray(inputs["L0_ffn_fc1_q"], np.int32)),
        "fc1_s": np.ascontiguousarray(np.asarray(inputs["L0_ffn_fc1_scales"], np.float32)),
        "fc2_q": np.ascontiguousarray(np.asarray(inputs["L0_ffn_fc2_q"], np.int32)),
        "fc2_s": np.ascontiguousarray(np.asarray(inputs["L0_ffn_fc2_scales"], np.float32)),
        "g0": _pp(inputs["L0_ffn_ln_gamma"]),
        "be0": _pp(inputs["L0_ffn_ln_beta"]),
        "b1t0": _pp(inputs["L0_ffn_fc1_bias"]),
        "b2r0": np.asarray(inputs["L0_ffn_fc2_bias"], np.float32).reshape(1, DM).copy(),
        "wq0": np.ascontiguousarray(np.asarray(inputs["L0_attn_Wq"], np.float32)),
        "wk0": np.ascontiguousarray(np.asarray(inputs["L0_attn_Wk"], np.float32)),
        "wv0": np.ascontiguousarray(np.asarray(inputs["L0_attn_Wv"], np.float32)),
    })
    in_maps1 = []
    for c in cores:
        mm = dict(shared1)
        mm["x_in"] = np.ascontiguousarray(x_full[_tiles_rows(c)])
        in_maps1.append(mm)
    r1 = run_bass_kernel_spmd(nc1, in_maps1, core_ids=cores).results

    ktf = {b: _assemble_ktf([r1[c]["kt0"] for c in cores], b) for b in range(B)}
    vf = {b: _assemble_vf([r1[c]["v0"] for c in cores], b, 16) for b in range(B)}

    nc2 = build_launch2()
    shared2 = dict(consts)
    shared2.update({
        "wo0": np.ascontiguousarray(np.asarray(inputs["L0_attn_Wo"], np.float32)),
        "bo0": np.asarray(inputs["L0_attn_bo"], np.float32).reshape(1, DM).copy(),
        "wdeq1": r1[0]["wdeq1"], "wdeq2": r1[0]["wdeq2"],
        "g0": shared1["g0"], "be0": shared1["be0"],
        "b1t0": shared1["b1t0"], "b2r0": shared1["b2r0"],
        "rngb": _pp(inputs["L1_ffn_rn_g"]), "rnbb": _pp(inputs["L1_ffn_rn_b"]),
        "wrt": np.ascontiguousarray(np.asarray(inputs["L1_ffn_router_w"], np.float32).T),
        "rb": np.asarray(inputs["L1_ffn_router_b"], np.float32).reshape(1, NEXP).copy(),
        "lng": np.ascontiguousarray(
            np.asarray(inputs["L1_ffn_ln_g"], np.float32).reshape(NEXP, NCH, 128)
            .transpose(0, 2, 1)),
        "lnb": np.ascontiguousarray(
            np.asarray(inputs["L1_ffn_ln_b"], np.float32).reshape(NEXP, NCH, 128)
            .transpose(0, 2, 1)),
        "w1": np.ascontiguousarray(np.asarray(inputs["L1_ffn_w1"], np.float32)),
        "b1T": np.ascontiguousarray(
            np.asarray(inputs["L1_ffn_b1"], np.float32).reshape(NEXP, 32, 128)
            .transpose(0, 2, 1)),
        "w2": np.ascontiguousarray(np.asarray(inputs["L1_ffn_w2"], np.float32)),
        "b2": np.ascontiguousarray(np.asarray(inputs["L1_ffn_b2"], np.float32)),
        "wq1": np.ascontiguousarray(
            np.asarray(inputs["L1_attn_w_aq"], np.float32).reshape(DM, DM)),
        "wk1": np.ascontiguousarray(
            np.asarray(inputs["L1_attn_w_ak"], np.float32).reshape(DM, DM)),
        "wv1": np.ascontiguousarray(
            np.asarray(inputs["L1_attn_w_av"], np.float32).reshape(DM, DM)),
    })
    in_maps2 = []
    for c in cores:
        mm = dict(shared2)
        b = c // 4
        mm["x_in"] = r1[c]["x2_out"]
        mm["qt_in"] = r1[c]["qt0"]
        mm["ktf"] = ktf[b]
        mm["vf"] = vf[b]
        mm["maskT"] = _build_masks(c)
        cosT, sinT = _build_rope(c)
        mm["cosT"] = cosT
        mm["sinT"] = sinT
        in_maps2.append(mm)
    r2 = run_bass_kernel_spmd(nc2, in_maps2, core_ids=cores).results

    ktf1 = {b: _assemble_ktf([r2[c]["kt1"] for c in cores], b) for b in range(B)}
    vf1 = {b: _assemble_vf([r2[c]["v1"] for c in cores], b, 8) for b in range(B)}

    nc3 = build_launch3()
    shared3 = dict(consts)
    shared3.update({
        "wo1": np.ascontiguousarray(
            np.asarray(inputs["L1_attn_w_ao"], np.float32).reshape(DM, DM)),
        "rngb": shared2["rngb"], "rnbb": shared2["rnbb"],
        "wrt": shared2["wrt"], "rb": shared2["rb"],
        "lng": shared2["lng"], "lnb": shared2["lnb"],
        "w1": shared2["w1"], "b1T": shared2["b1T"],
        "w2": shared2["w2"], "b2": shared2["b2"],
    })
    in_maps3 = []
    for c in cores:
        mm = dict(shared3)
        b = c // 4
        mm["x_in"] = r2[c]["x5_out"]
        mm["qt_in"] = r2[c]["qt1"]
        mm["ktf"] = ktf1[b]
        mm["vf"] = vf1[b]
        mm["maskT"] = _build_masks(c)
        in_maps3.append(mm)
    r3 = run_bass_kernel_spmd(nc3, in_maps3, core_ids=cores).results

    out = np.empty((B, SEQ, DM), dtype=np.float32)
    flat = out.reshape(B * SEQ, DM)
    for c in cores:
        flat[_tiles_rows(c)] = r3[c]["x_out"]
    return out
